# version 02 (final)
# baseline (speedup 1.0000x reference)
import math
import numpy as np
import ml_dtypes
import concourse.bass as bass
import concourse.mybir as mybir
from concourse.bass_utils import run_bass_kernel_spmd
from contextlib import ExitStack

F32 = mybir.dt.float32
BF16 = mybir.dt.bfloat16
I8 = mybir.dt.int8
F16 = mybir.dt.float16
AF = mybir.ActivationFunctionType
ALU = mybir.AluOpType
AX = mybir.AxisListType.X

T = 2048
C = 1024
CH = 8
NH = 16
DH = 64
BW = 2050
YW = 2049
D1 = 2730
NJ = 22
JL = 42
DEPTH = 6
NEG = -1.0e9
EPS_LA = 1.1920929e-07
S8 = math.sqrt(8.0)


def _host_tables():
    half = np.arange(0, DH, 2, dtype=np.float64)
    inv_freq = 1.0 / 10000.0 ** (half / DH)
    t = np.arange(T, dtype=np.float64)
    freqs = t[:, None] * inv_freq[None, :]
    freqs = np.concatenate([freqs, freqs], -1)
    base_scale = (half + 0.4 * DH) / (1.4 * DH)
    power = (t - T // 2) / 128.0
    scale = base_scale[None, :] ** power[:, None]
    scale = np.concatenate([scale, scale], -1)
    cos, sin = np.cos(freqs), np.sin(freqs)
    inv = 1.0 / scale
    sgn = np.concatenate([-np.ones(32), np.ones(32)])[None, :]
    Aq = cos * scale * S8
    Bq = sin * scale * S8 * sgn
    Ak = cos * inv * S8
    Bk = sin * inv * S8 * sgn
    tbl = np.stack([Aq, Bq, Ak, Bk], 0)
    tbl = tbl.reshape(4, 16, 128, DH).transpose(2, 0, 1, 3)
    t16 = tbl.astype(np.float16)
    assert np.isfinite(t16.astype(np.float64)).all()
    return np.ascontiguousarray(t16)


def _host_masks():
    i = np.arange(128)[:, None]
    j = np.arange(128)[None, :]
    mA = np.where(j >= i, 0.0, NEG)
    mC = np.where(j <= i, 0.0, NEG)
    mZ = np.zeros((128, 128))
    m = np.stack([mA, mC, mZ], 1).astype(np.float32)
    return np.ascontiguousarray(m.astype(ml_dtypes.bfloat16))


class SmallPack:
    def __init__(self):
        self.cols = {}
        self.buf = []

    def add(self, name, v):
        v = np.asarray(v, np.float32).ravel()
        ncol = (v.size + 127) // 128
        full = np.zeros(ncol * 128, np.float32)
        full[: v.size] = v
        self.cols[name] = len(self.buf)
        for k in range(ncol):
            self.buf.append(full[k * 128:(k + 1) * 128])

    def array(self):
        return np.ascontiguousarray(np.stack(self.buf, 1))


def _host_smallpack(inp):
    sp = SmallPack()
    sp.add("l0_bout", inp["L0_attn_b_out"])
    sp.add("l0_g", inp["L0_attn_ln_gamma"])
    sp.add("l0_b", inp["L0_attn_ln_beta"])
    sp.add("l0_fbias", inp["L0_ffn_bias"])
    sp.add("l1_bproj", inp["L1_attn_b_proj"])
    bq = np.asarray(inp["L0_attn_b_qkv"], np.float32)
    qh = np.zeros((NH, 128), np.float32)
    kh = np.zeros((NH, 128), np.float32)
    for h in range(NH):
        qh[h, :DH] = bq[h * DH:(h + 1) * DH]
        kh[h, :DH] = bq[C + h * DH: C + (h + 1) * DH]
    sp.cols["l0_bq_h"] = len(sp.buf)
    for h in range(NH):
        sp.buf.append(qh[h])
    sp.cols["l0_bk_h"] = len(sp.buf)
    for h in range(NH):
        sp.buf.append(kh[h])
    for i in range(DEPTH):
        sp.add(f"g{i}", inp["L1_ffn_ln_g"][i])
        sp.add(f"b{i}", inp["L1_ffn_ln_b"][i])
        b1 = np.asarray(inp["L1_ffn_b1"][i], np.float32)
        ap = np.zeros(NJ * 128, np.float32)
        ap[:D1] = b1[:D1]
        gp = np.zeros(NJ * 128, np.float32)
        gp[:D1] = b1[D1:]
        sp.add(f"b1a{i}", ap)
        sp.add(f"b1g{i}", gp)
        sp.add(f"b2_{i}", inp["L1_ffn_b2"][i])
    return sp.array(), sp.cols


def _host_qs(inp):
    qs = np.asarray(inp["L1_ffn_q_scale"], np.float32)
    ks = np.asarray(inp["L1_ffn_k_scale"], np.float32)
    a = np.stack([qs, ks], 1)[None].repeat(128, 0)
    return np.ascontiguousarray(a)


class MK:
    def __init__(self, stage, nlayers, sp_ncol, spc):
        self.stage = stage
        self.nlayers = nlayers
        self.sp_ncol = sp_ncol
        self.SPC = spc
        nc = bass.Bass("TRN2", target_bir_lowering=False, detect_race_conditions=False)
        self.nc = nc
        self.ctx = ExitStack()
        self._sems = {}
        self.V, self.S, self.T, self.SY, self.GP = nc.vector, nc.scalar, nc.tensor, nc.sync, nc.gpsimd
        self.declare_io()
        self.alloc_sbuf()

    def sem(self, name):
        if name not in self._sems:
            self._sems[name] = [self.ctx.enter_context(self.nc.semaphore(name)), 0]
        return self._sems[name]

    def inc(self, instr, name, k=1):
        s = self.sem(name)
        instr.then_inc(s[0], k)
        s[1] += k
        return s[1]

    def n(self, name):
        return self.sem(name)[1]

    def wait(self, eng, name, v=None):
        s = self.sem(name)
        if v is None:
            v = s[1]
        assert v <= s[1], f"wait beyond produced: {name} {v} > {s[1]}"
        if v > 0:
            eng.wait_ge(s[0], v)

    def gate(self):
        self.nc.all_engine_barrier()

    def vs(self, instr):
        tk = self.inc(instr, "sv", 1)
        self.wait(self.V, "sv", tk)
        return tk

    def pt(self, px, name, shape, dtype=F32):
        self._ptn = getattr(self, "_ptn", 0) + 1
        return px.enter_context(self.nc.psum_tensor(f"{name}{self._ptn}", shape, dtype))

    def declare_io(self):
        nc = self.nc
        self.xT = nc.dram_tensor("xT", [C, T], F32, kind="ExternalInput").ap()
        self.out = nc.dram_tensor("out", [C, T], F32, kind="ExternalOutput").ap()
        self.d_l0_wqkv = nc.dram_tensor("l0_wqkv", [C, 3 * C], F32, kind="ExternalInput").ap()
        self.d_l0_wout = nc.dram_tensor("l0_wout", [C, C], F32, kind="ExternalInput").ap()
        self.d_l0_fwT = nc.dram_tensor("l0_fwT", [C, C], F32, kind="ExternalInput").ap()
        self.d_l1_wqkvT = nc.dram_tensor("l1_wqkvT", [C, 3 * C], F32, kind="ExternalInput").ap()
        self.d_l1_wprojT = nc.dram_tensor("l1_wprojT", [C, C], F32, kind="ExternalInput").ap()
        self.d_wqkv = nc.dram_tensor("f_wqkv", [DEPTH, C, 3 * C], F32, kind="ExternalInput").ap()
        self.d_wo = nc.dram_tensor("f_wo", [DEPTH, C, C], F32, kind="ExternalInput").ap()
        self.d_w1 = nc.dram_tensor("f_w1", [DEPTH, C, 2 * D1], F32, kind="ExternalInput").ap()
        self.d_w2 = nc.dram_tensor("f_w2", [DEPTH, D1, C], F32, kind="ExternalInput").ap()
        self.d_tbl = nc.dram_tensor("tbl", [128, 4, 16, DH], F16, kind="ExternalInput").ap()
        self.d_e8 = nc.dram_tensor("e8", [1, 64], F32, kind="ExternalInput").ap()
        self.d_masks = nc.dram_tensor("masks", [128, 3, 128], BF16, kind="ExternalInput").ap()
        self.d_identb = nc.dram_tensor("identb", [128, 128], BF16, kind="ExternalInput").ap()
        self.d_identf = nc.dram_tensor("identf", [128, 128], F32, kind="ExternalInput").ap()
        self.d_sp = nc.dram_tensor("sp", [128, self.sp_ncol], F32, kind="ExternalInput").ap()
        self.d_qs = nc.dram_tensor("qsks", [128, DEPTH, 2, DH], F32, kind="ExternalInput").ap()
        self.d_bv = nc.dram_tensor("l0_bv", [1, C], F32, kind="ExternalInput").ap()
        self.xsave = nc.dram_tensor("xsave", [C, T], F32, kind="Internal")
        self.cc_in = nc.dram_tensor("cc_in", [1, 8], F32, kind="Internal")
        self.cc_out = nc.dram_tensor("cc_out", [1, 8], F32, kind="Internal")

    def alloc_sbuf(self):
        nc, ctx = self.nc, self.ctx
        self.Y = ctx.enter_context(nc.sbuf_tensor("Y", [128, CH * YW], F32))
        self.YBF = ctx.enter_context(nc.sbuf_tensor("YBF", [128, CH * BW], BF16))
        self.ABUF = ctx.enter_context(nc.sbuf_tensor("ABUF", [128, 8192], BF16))
        self.SHD = ctx.enter_context(nc.sbuf_tensor("SHD", [128, 16384], BF16))
        self.WBUF = ctx.enter_context(nc.sbuf_tensor("WBUF", [128, 8192], BF16))
        self.TBL = ctx.enter_context(nc.sbuf_tensor("TBL", [128, 4 * 16 * DH], F16))
        self.RB = ctx.enter_context(nc.sbuf_tensor("RB", [1, 4096], F32))
        self.PSTAT = ctx.enter_context(nc.sbuf_tensor("PSTAT", [128, 128], F32))
        self.XS = ctx.enter_context(nc.sbuf_tensor("XS", [128, 1024], F32))
        self.QS = ctx.enter_context(nc.sbuf_tensor("QS", [128, 1024], F32))
        self.QSK = ctx.enter_context(nc.sbuf_tensor("QSK", [128, DEPTH * 2 * DH], F32))
        self.SPK = ctx.enter_context(nc.sbuf_tensor("SPK", [128, self.sp_ncol], F32))
        self.MASKS = ctx.enter_context(nc.sbuf_tensor("MASKS", [128, 384], BF16))
        self.IDB = ctx.enter_context(nc.sbuf_tensor("IDB", [128, 128], BF16))
        self.IDF = ctx.enter_context(nc.sbuf_tensor("IDF", [128, 128], F32))
        self.ONEB = ctx.enter_context(nc.sbuf_tensor("ONEB", [128, 8], BF16))
        self.ONEF = ctx.enter_context(nc.sbuf_tensor("ONEF", [128, 8], F32))
        self.E8 = ctx.enter_context(nc.sbuf_tensor("E8", [1, 64], F32))
        self.CEL = ctx.enter_context(nc.sbuf_tensor("CEL", [1, 16], F32))
        self.ONER = ctx.enter_context(nc.sbuf_tensor("ONER", [1, 128], BF16))
        self.ONERF = ctx.enter_context(nc.sbuf_tensor("ONERF", [1, 128], F32))
        self.SMALL = ctx.enter_context(nc.sbuf_tensor("SMALL", [128, 64], F32))
        self.CNST = ctx.enter_context(nc.sbuf_tensor("CNST", [128, 40], F32))
        self.PTS = ctx.enter_context(nc.sbuf_tensor("PTS", [128, 512], BF16))
        self.QI8 = ctx.enter_context(nc.sbuf_tensor("QI8", [128, 512], I8))
        self.GEL = ctx.enter_context(nc.sbuf_tensor("GEL", [128, 512], BF16))
        self.BVR = ctx.enter_context(nc.sbuf_tensor("BVR", [1, C], BF16))
        self.KSB = ctx.enter_context(nc.sbuf_tensor("KSB", [128, 8], BF16))
        self.CTXB = ctx.enter_context(nc.sbuf_tensor("CTXB", [128, 64], BF16))
        self.tblv = self.TBL[:].rearrange("p (a c d) -> p a c d", a=4, c=16)
        self.qskv = self.QSK[:].rearrange("p (l a d) -> p l a d", l=DEPTH, a=2)
        self.maskv = self.MASKS[:].rearrange("p (a j) -> p a j", a=3)
        assert nc.sbuf_bytes_remaining >= 0, nc.sbuf_bytes_remaining

    def ych(self, c):
        return self.Y[:, c * YW:(c + 1) * YW]

    def ybf(self, c):
        return self.YBF[:, c * BW:(c + 1) * BW]

    def spcol(self, name, k=0, p0=0, p1=128):
        c = self.SPC[name] + k
        return self.SPK[p0:p1, c:c + 1]

    def rb(self, seg, w=512):
        return self.RB[0:1, seg * 512: seg * 512 + w]

    def rb1(self, col):
        return self.RB[0:1, col:col + 1]

    PK_MU1, PK_IV1, PK_MU2, PK_IV2 = 0, 32, 64, 96

    def pack_row(self, px, rb_seg, dst_base, g):
        TE, V = self.T, self.V
        ps = self.ps_pack
        for i in range(4):
            mm = TE.matmul(ps[:, 0:8], self.RB[0:1, rb_seg * 512 + i * 128: rb_seg * 512 + (i + 1) * 128],
                           self.E8[0:1, 8 * i: 8 * i + 8], start=(i == 0), stop=(i == 3))
        tk = self.inc(mm, "st", 1)
        self.wait(V, "st", tk)
        tkV = self.inc(V.tensor_copy(self.PSTAT[:, dst_base + 8 * g: dst_base + 8 * g + 8], ps[:, 0:8]),
                       "sv", 1)
        self.wait(TE, "sv", tkV)

    def unpack_row(self, px, src_base, g, rb_seg):
        TE, V = self.T, self.V
        ps = self.ps_row
        for i in range(4):
            mm = TE.transpose(ps[0:1, i * 128:(i + 1) * 128],
                              self.PSTAT[:, src_base + 8 * g + i: src_base + 8 * g + i + 1], self.IDF[:])
        tk = self.inc(mm, "st", 1)
        self.wait(V, "st", tk)
        tkV = self.inc(V.tensor_copy(self.rb(rb_seg), ps[0:1, 0:512]), "sv", 1)
        self.wait(TE, "sv", tkV)

    def bcast(self, rb_seg, dst_ps, m=128, w=512):
        mm = self.T.matmul(dst_ps[0:m, 0:w], self.ONERF[0:1, 0:m], self.rb(rb_seg, w), start=True, stop=True)
        return self.inc(mm, "st", 1)

    def initial_loads(self):
        SY, V, GP = self.SY, self.V, self.GP
        tkm = self.inc(V.memset(self.Y[:], 0.0), "sv", 1)
        self.inc(V.memset(self.YBF[:], 0.0), "sv", 1)
        self.wait(SY, "sv", tkm)
        self.inc(SY.dma_start(out=self.Y[:].rearrange("p (c w) -> p c w", c=CH)[:, :, 1:],
                              in_=self.xT.rearrange("(c p) t -> p c t", p=128)), "sl", 16)
        self.inc(SY.dma_start(out=self.E8[:], in_=self.d_e8), "sl", 16)
        self.inc(SY.dma_start(out=self.TBL[:], in_=self.d_tbl.rearrange("p a c d -> p (a c d)")), "sl", 16)
        self.inc(SY.dma_start(out=self.MASKS[:], in_=self.d_masks.rearrange("p a j -> p (a j)")), "sl", 16)
        self.inc(SY.dma_start(out=self.IDB[:], in_=self.d_identb), "sl", 16)
        self.inc(SY.dma_start(out=self.IDF[:], in_=self.d_identf), "sl", 16)
        self.inc(SY.dma_start(out=self.SPK[:], in_=self.d_sp), "sl", 16)
        self.inc(GP.dma_start(out=self.BVR[:], in_=self.d_bv), "sg", 16)
        self.inc(SY.dma_start(out=self.QSK[:], in_=self.d_qs.rearrange("p l a d -> p (l a d)")), "sl", 16)
        tok = self.n("sl")
        self.wait(V, "sl", tok)
        V.memset(self.ONEB[:], 1.0)
        V.memset(self.ONEF[:], 1.0)
        V.memset(self.ONER[:], 1.0)
        V.memset(self.ONERF[:], 1.0)
        V.memset(self.CNST[:, 0:8], 1e-5)
        V.memset(self.CNST[:, 8:16], 1e-30)
        V.memset(self.CNST[:, 16:24], EPS_LA)
        V.memset(self.CNST[:, 24:32], 1e-24)
        self.inc(V.memset(self.CNST[:, 32:40], math.log(1.05)), "sv", 1)
        self.gate()

    def ln_stats(self, mu_base, iv_base, shifted):
        nc, V, S, TE = self.nc, self.V, self.S, self.T
        with ExitStack() as px:
            ps_s = self.pt(px, "lns", [1, 512], F32)
            ps_q = self.pt(px, "lnq", [1, 512], F32)
            self.ps_pack = self.pt(px, "lpk", [128, 8], F32)
            for g in range(4):
                lo = g * 512
                for ci in range(CH):
                    off = 0 if (shifted and ci >= 4) else 1
                    ysl = self.ych(ci)[:, off + lo: off + lo + 512]
                    self.wait(S, "st")
                    tkS = self.inc(S.activation(self.XS[:, 0:512], ysl, AF.Square), "ss", 1)
                    self.wait(TE, "ss", tkS)
                    TE.matmul(ps_s[:], self.ONEF[:, 0:1], ysl, start=(ci == 0), stop=(ci == CH - 1))
                    mm = TE.matmul(ps_q[:], self.ONEF[:, 0:1], self.XS[:, 0:512],
                                   start=(ci == 0), stop=(ci == CH - 1))
                    self.inc(mm, "st", 1)
                self.wait(V, "st")
                V.tensor_scalar(self.rb(0), ps_s[:], 1.0 / C, None, ALU.mult)
                tkV = self.inc(V.tensor_scalar(self.rb(1), ps_q[:], 1.0 / C, None, ALU.mult), "sv", 1)
                self.wait(TE, "sv", tkV)
                V.tensor_tensor(self.rb(2), self.rb(0), self.rb(0), ALU.mult)
                tkV = self.inc(V.tensor_tensor(self.rb(1), self.rb(1), self.rb(2), ALU.subtract), "sv", 1)
                self.wait(S, "sv", tkV)
                S.activation(self.rb(1), self.rb(1), AF.Ln, bias=self.CNST[0:1, 0:1], scale=1.0)
                tkS = self.inc(S.activation(self.rb(1), self.rb(1), AF.Exp, bias=0.0, scale=-0.5), "ss", 1)
                self.wait(TE, "ss", tkS)
                self.pack_row(px, 0, mu_base, g)
                self.pack_row(px, 1, iv_base, g)
        self.gate()

    def ln_apply_bcast(self, px, g, mu_base, iv_base, bc_i, bc_c):
        V = self.V
        self.unpack_row(px, mu_base, g, 0)
        self.unpack_row(px, iv_base, g, 1)
        V.tensor_tensor(self.rb(2), self.rb(0), self.rb(1), ALU.mult)
        tkV = self.inc(V.tensor_scalar(self.rb(2), self.rb(2), -1.0, None, ALU.mult), "sv", 1)
        self.wait(self.T, "sv", tkV)
        self.bcast(1, bc_i)
        return self.bcast(2, bc_c)

    def phase1(self):
        nc, V, S, TE = self.nc, self.V, self.S, self.T
        self.ln_stats(self.PK_MU1, self.PK_IV1, shifted=False)
        with ExitStack() as px:
            bc_i = self.pt(px, "bcA", [128, 512], F32)
            bc_c = self.pt(px, "bcB", [128, 512], F32)
            st_s = self.pt(px, "s2s", [1, 512], F32)
            st_q = self.pt(px, "s2q", [1, 512], F32)
            self.ps_pack = self.pt(px, "lpk", [128, 8], F32)
            self.ps_row = self.pt(px, "lpr", [1, 512], F32)
            for g in range(4):
                lo = g * 512
                self.wait(TE, "sv")
                tkT = self.ln_apply_bcast(px, g, self.PK_MU1, self.PK_IV1, bc_i, bc_c)
                for ci in range(CH):
                    self.wait(V, "st", tkT)
                    self.wait(V, "st")
                    V.tensor_tensor(self.XS[:, 0:512], self.ych(ci)[:, 1 + lo:1 + lo + 512], bc_i[:], ALU.mult)
                    V.tensor_tensor(self.XS[:, 0:512], self.XS[:, 0:512], bc_c[:], ALU.add)
                    V.tensor_copy(self.ybf(ci)[:, 1 + lo:1 + lo + 512], self.XS[:, 0:512])
                    tkV = self.inc(V.tensor_reduce(self.SMALL[:, 32 + g * 8 + ci: 33 + g * 8 + ci],
                                                   self.XS[:, 0:512], AX, ALU.max,
                                                   apply_absolute_value=True), "sv", 1)
                    self.wait(S, "sv", tkV)
                    tkS = self.inc(S.activation(self.XS[:, 512:1024], self.XS[:, 0:512], AF.Square), "ss", 1)
                    self.wait(TE, "ss", tkS)
                    TE.matmul(st_s[:], self.ONEF[:, 0:1], self.XS[:, 0:512], start=(ci == 0), stop=(ci == CH - 1))
                    mm = TE.matmul(st_q[:], self.ONEF[:, 0:1], self.XS[:, 512:1024],
                                   start=(ci == 0), stop=(ci == CH - 1))
                    self.inc(mm, "st", 1)
                self.wait(V, "st")
                V.tensor_scalar(self.rb(0), st_s[:], 1.0 / C, None, ALU.mult)
                tkV = self.inc(V.tensor_scalar(self.rb(1), st_q[:], 1.0 / C, None, ALU.mult), "sv", 1)
                self.wait(TE, "sv", tkV)
                V.tensor_tensor(self.rb(2), self.rb(0), self.rb(0), ALU.mult)
                tkV = self.inc(V.tensor_tensor(self.rb(1), self.rb(1), self.rb(2), ALU.subtract), "sv", 1)
                self.wait(S, "sv", tkV)
                S.activation(self.rb(1), self.rb(1), AF.Ln, bias=self.CNST[0:1, 0:1], scale=1.0)
                tkS = self.inc(S.activation(self.rb(1), self.rb(1), AF.Exp, bias=0.0, scale=-0.5), "ss", 1)
                self.wait(TE, "ss", tkS)
                self.pack_row(px, 0, self.PK_MU2, g)
                self.pack_row(px, 1, self.PK_IV2, g)
            self.wait(V, "sv")
            self.inc(V.tensor_reduce(self.SMALL[:, 24:25], self.SMALL[:, 32:64], AX, ALU.max), "sv", 1)
        self.gate()

    def dump_xn(self):
        nc, V, SY, TE = self.nc, self.V, self.SY, self.T
        ov = self.out.rearrange("(c p) t -> p c t", p=128)
        with ExitStack() as px:
            bc_i = self.pt(px, "bcA", [128, 512], F32)
            bc_c = self.pt(px, "bcB", [128, 512], F32)
            self.ps_row = self.pt(px, "lpr", [1, 512], F32)
            for g in range(4):
                lo = g * 512
                self.wait(TE, "sv")
                self.wait(TE, "so")
                tkT = self.ln_apply_bcast(px, g, self.PK_MU1, self.PK_IV1, bc_i, bc_c)
                for ci in range(CH):
                    self.wait(V, "st", tkT)
                    self.wait(V, "so")
                    V.tensor_tensor(self.XS[:, 0:512], self.ych(ci)[:, 1 + lo:1 + lo + 512], bc_i[:], ALU.mult)
                    tkV = self.inc(V.tensor_tensor(self.XS[:, 0:512], self.XS[:, 0:512], bc_c[:], ALU.add),
                                   "sv", 1)
                    self.wait(SY, "sv", tkV)
                    self.inc(SY.dma_start(out=ov[:, ci, lo:lo + 512], in_=self.XS[:, 0:512]), "so", 16)
                    self.wait(SY, "so")
        self.SY.wait_ge(self.sem("so")[0], self.n("so"))

    def phase2_h(self):
        nc, V, S, TE = self.nc, self.V, self.S, self.T
        HB = self.SHD[:, 0:16384].rearrange("p (c t) -> p c t", c=CH)
        self.HB = HB
        with ExitStack() as px:
            bc_a = self.pt(px, "bcA", [128, 512], F32)
            bc_b = self.pt(px, "bcB", [128, 512], F32)
            self.ps_row = self.pt(px, "lpr", [1, 512], F32)
            for g in range(4):
                lo = g * 512
                self.wait(TE, "ss")
                self.wait(TE, "sv")
                self.unpack_row(px, self.PK_MU1, g, 0)
                self.unpack_row(px, self.PK_IV1, g, 1)
                self.unpack_row(px, self.PK_MU2, g, 2)
                self.unpack_row(px, self.PK_IV2, g, 3)
                V.tensor_tensor(self.rb(1), self.rb(1), self.rb(3), ALU.mult)
                V.tensor_tensor(self.rb(0), self.rb(0), self.rb(1), ALU.mult)
                V.tensor_tensor(self.rb(2), self.rb(2), self.rb(3), ALU.mult)
                V.tensor_tensor(self.rb(0), self.rb(0), self.rb(2), ALU.add)
                tkV = self.inc(V.tensor_scalar(self.rb(0), self.rb(0), -1.0, None, ALU.mult), "sv", 1)
                self.wait(TE, "sv", tkV)
                self.bcast(1, bc_a)
                tkT = self.bcast(0, bc_b)
                for ci in range(CH):
                    self.wait(V, "st", tkT)
                    self.wait(V, "ss")
                    V.tensor_tensor(self.XS[:, 0:512], self.ych(ci)[:, 1 + lo:1 + lo + 512], bc_a[:], ALU.mult)
                    tkV = self.inc(V.tensor_tensor(self.XS[:, 0:512], self.XS[:, 0:512], bc_b[:], ALU.add),
                                   "sv", 1)
                    self.wait(S, "sv", tkV)
                    self.inc(S.activation(HB[:, ci, lo:lo + 512], self.XS[:, 0:512], AF.Identity,
                                          bias=self.spcol("l0_b", ci), scale=self.spcol("l0_g", ci)), "ss", 1)
                self.wait(TE, "ss")
        self.gate()

    def phase3_l0_attn(self):
        nc, V, S, TE, GP = self.nc, self.V, self.S, self.T, self.GP
        HB = self.HB
        AB = self.ABUF
        QAUG = AB[:, 0:2048]
        KAUG = AB[:, 2048:4096]
        VAUG = AB[:, 4096:5136].rearrange("p (c v) -> p c v", c=16)
        PT = [AB[:, 5136:7184], self.WBUF[:, 1536:3584]]
        wqv = self.WBUF[:, 0:1536].rearrange("p (w k d) -> p w k d", w=3, k=8)
        wsrc = self.d_l0_wqkv.rearrange("(k p) f -> p k f", p=128)

        for h in range(NH):
            with ExitStack() as px:
                psq = self.pt(px, "l0q", [128, 512], F32)
                psn = self.pt(px, "l0n", [1, 512], F32)
                self.wait(GP, "st")
                for w, col0 in ((0, h * DH), (1, C + h * DH), (2, 2 * C + h * DH)):
                    self.inc(GP.dma_start(out=wqv[:, w, :, :], in_=wsrc[:, :, col0:col0 + DH]), "sg", 16)
                tw = self.n("sg")
                self.wait(TE, "sg", tw)
                for w, dst, bname in ((0, QAUG, "l0_bq_h"), (1, KAUG, "l0_bk_h")):
                    for g in range(4):
                        self.wait(TE, "sv")
                        for k in range(CH):
                            mm = TE.matmul(psq[0:DH, :], wqv[:, w, k, :], HB[:, k, g * 512:(g + 1) * 512],
                                           start=(k == 0), stop=(k == CH - 1))
                        tkT = self.inc(mm, "st", 1)
                        self.wait(V, "st", tkT)
                        self.inc(V.tensor_scalar(dst[0:DH, g * 512:(g + 1) * 512], psq[0:DH, :],
                                                 self.spcol(bname, h, 0, DH), None, ALU.add), "sv", 1)
                for tc in range(16):
                    self.wait(TE, "sv")
                    TE.matmul(psq[:, 0:DH], self.ONER[:], self.BVR[0:1, h * DH:(h + 1) * DH],
                              start=True, stop=False)
                    for k in range(CH):
                        mm = TE.matmul(psq[:, 0:DH], HB[:, k, tc * 128:(tc + 1) * 128],
                                       wqv[:, 2, k, :], start=False, stop=(k == CH - 1))
                    tkT = self.inc(mm, "st", 1)
                    self.wait(V, "st", tkT)
                    self.inc(V.tensor_copy(VAUG[:, tc, 0:DH], psq[:, 0:DH]), "sv", 1)
                V.memset(VAUG[:, :, DH:DH + 1], 1.0)
                tkV = self.inc(V.memset(KAUG[DH:DH + 1, :], 1.0), "sv", 1)
                self.wait(S, "sv", tkV)
                tkS = self.inc(S.activation(PT[0][0:DH, :], KAUG[0:DH, :], AF.Square), "ss", 1)
                self.wait(TE, "ss", tkS)
                for g in range(4):
                    mm = TE.matmul(psn[:], self.ONEB[0:DH, 0:1], PT[0][0:DH, g * 512:(g + 1) * 512],
                                   start=True, stop=True)
                    tkT = self.inc(mm, "st", 1)
                    self.wait(V, "st", tkT)
                    tkV = self.inc(V.tensor_copy(self.rb(g) if g < 3 else self.RB[0:1, 1536:2048], psn[:]),
                                   "sv", 1)
                    self.wait(TE, "sv", tkV)
                tkV = self.vs(V.tensor_reduce(self.CEL[0:1, 2:3], self.RB[0:1, 0:2048], AX,
                                              ALU.max))
                self.wait(S, "sv", tkV)
                S.activation(self.CEL[0:1, 2:3], self.CEL[0:1, 2:3], AF.Ln,
                             bias=self.CNST[0:1, 8:9], scale=1.0)
                tkS = self.inc(S.activation(self.CEL[0:1, 2:3], self.CEL[0:1, 2:3], AF.Exp,
                                            bias=self.CNST[0:1, 32:33], scale=0.5), "ss", 1)
                self.wait(S, "sv", tkV)
                tkS = self.inc(S.activation(PT[0][0:DH, :], QAUG[0:DH, :], AF.Square), "ss", 1)
                self.wait(TE, "ss", tkS)
                for g in range(4):
                    mm = TE.matmul(psn[:], self.ONEB[0:DH, 0:1], PT[0][0:DH, g * 512:(g + 1) * 512],
                                   start=True, stop=True)
                    tkT = self.inc(mm, "st", 1)
                    self.wait(V, "st", tkT)
                    tkV = self.inc(V.tensor_copy(self.rb(2), psn[:]), "sv", 1)
                    self.wait(S, "sv", tkV)
                    S.activation(self.rb(2), self.rb(2), AF.Ln, bias=self.CNST[0:1, 8:9], scale=1.0)
                    tkS = self.inc(S.activation(self.rb(2), self.rb(2), AF.Exp, bias=0.0, scale=0.5), "ss", 1)
                    self.wait(V, "ss", tkS)
                    tkV = self.inc(V.tensor_scalar(self.rb(2), self.rb(2), self.CEL[0:1, 2:3], 8.0,
                                                   ALU.mult, ALU.add), "sv", 1)
                    self.wait(S, "sv", tkV)
                    tkS = self.inc(S.activation(QAUG[DH:DH + 1, g * 512:(g + 1) * 512], self.rb(2), AF.Copy,
                                                bias=0.0, scale=-1.0), "ss", 1)
                    self.wait(TE, "ss", tkS)
            with ExitStack() as px:
                stA = self.pt(px, "l0sA", [128, 1024], F32)
                stB = self.pt(px, "l0sB", [128, 1024], F32)
                pv = self.pt(px, "l0pv", [DH + 1, 2048], F32)
                exp_tok = [[0, 0] for _ in range(16)]
                pv_tok = [0] * 16
                for ktc in range(16):
                    ptb = PT[ktc % 2]
                    if ktc >= 2:
                        self.wait(S, "st", pv_tok[ktc - 2])
                    for hf in range(2):
                        stp = stA if hf == 0 else stB
                        if ktc >= 1:
                            self.wait(TE, "ss", exp_tok[ktc - 1][hf])
                        for g2 in range(2):
                            g = hf * 2 + g2
                            mm = TE.matmul(stp[:, g2 * 512:(g2 + 1) * 512],
                                           KAUG[0:DH + 1, ktc * 128:(ktc + 1) * 128],
                                           QAUG[0:DH + 1, g * 512:(g + 1) * 512], start=True, stop=True)
                        tkT = self.inc(mm, "st", 1)
                        self.wait(S, "st", tkT)
                        ev = self.inc(S.activation(ptb[:, hf * 1024:(hf + 1) * 1024], stp[:, 0:1024],
                                                   AF.Exp, bias=0.0, scale=0.125), "ss", 1)
                        exp_tok[ktc][hf] = ev
                    self.wait(TE, "ss", exp_tok[ktc][1])
                    for g in range(4):
                        mm = TE.matmul(pv[:, g * 512:(g + 1) * 512], VAUG[:, ktc, :],
                                       ptb[:, g * 512:(g + 1) * 512], start=(ktc == 0), stop=(ktc == 15))
                    pv_tok[ktc] = self.inc(mm, "st", 1)
                self.wait(S, "st")
                ci, r0 = h // 2, (h % 2) * DH
                for g in range(4):
                    self.wait(S, "sv")
                    S.activation(self.rb(2), pv[DH:DH + 1, g * 512:(g + 1) * 512], AF.Ln,
                                 bias=self.CNST[0:1, 8:9], scale=1.0)
                    tkS = self.inc(S.activation(self.rb(2), self.rb(2), AF.Exp, bias=0.0, scale=-1.0),
                                   "ss", 1)
                    self.wait(TE, "ss", tkS)
                    mm = TE.matmul(stA[0:DH, 0:512], self.ONERF[0:1, 0:DH], self.rb(2), start=True, stop=True)
                    tkT = self.inc(mm, "st", 1)
                    self.wait(V, "st", tkT)
                    V.tensor_copy(self.XS[0:DH, 0:512], stA[0:DH, 0:512])
                    tkV = self.inc(V.tensor_tensor(
                        self.YBF[r0:r0 + DH, ci * BW + 1 + g * 512: ci * BW + 1 + (g + 1) * 512],
                        pv[0:DH, g * 512:(g + 1) * 512], self.XS[0:DH, 0:512], ALU.mult), "sv", 1)
                    self.wait(TE, "sv", tkV)
        self.gate()
        with ExitStack() as px:
            pso = [self.pt(px, "wo", [128, 512], F32) for i in range(2)]
            wom = self.WBUF[:, 0:2048].rearrange("p (b k m) -> p b k m", b=2, k=8)
            use_tok = [0] * CH
            wsrc2 = self.d_l0_wout.rearrange("(k p) f -> p k f", p=128)
            for m in range(CH):
                if m >= 2:
                    self.wait(GP, "st", use_tok[m - 2])
                ld = self.inc(GP.dma_start(out=wom[:, m % 2, :, :], in_=wsrc2[:, :, m * 128:(m + 1) * 128]), "sg", 16)
                self.wait(TE, "sg", ld)
                for g in range(4):
                    ps = pso[g % 2]
                    self.wait(TE, "sv")
                    for k in range(CH):
                        mm = TE.matmul(ps[:], wom[:, m % 2, k, :],
                                       self.YBF[:, k * BW + 1 + g * 512: k * BW + 1 + (g + 1) * 512],
                                       start=(k == 0), stop=(k == CH - 1))
                    tkT = self.inc(mm, "st", 1)
                    use_tok[m] = tkT
                    self.wait(V, "st", tkT)
                    tkV = self.inc(V.scalar_tensor_tensor(
                        self.ych(m)[:, 1 + g * 512:1 + (g + 1) * 512], ps[:], self.spcol("l0_bout", m),
                        self.ych(m)[:, 1 + g * 512:1 + (g + 1) * 512], ALU.add, ALU.add), "sv", 1)
                    self.wait(TE, "sv", tkV)
        self.gate()

    def phase4_l0_ffn(self):
        nc, V, S, TE, GP, SY = self.nc, self.V, self.S, self.T, self.GP, self.SY
        SM_IMAX, SM_T0, SM_RQ, SM_MIW, SM_THR, SM_ACC = 0, 1, 2, 3, 4, 5
        wsrc = self.d_l0_fwT.rearrange("(c p) f -> p c f", p=128)
        with ExitStack() as px:
            pst = self.pt(px, "tr1", [1, 512], F32)
            psb = self.pt(px, "tr2", [128, 512], F32)
            R0, R1, R2, R3, R4, R5 = 3072, 3080, 3088, 3096, 3104, 3112
            tkT = self.inc(TE.transpose(pst[0:1, 0:128], self.SMALL[:, 24:25], self.IDF[:]), "st", 1)
            self.wait(V, "st", tkT)
            self.vs(V.tensor_reduce(self.CEL[0:1, 0:1], pst[0:1, 0:128], AX, ALU.max))
            tkV = self.vs(V.tensor_scalar(self.RB[0:1, R1:R1 + 8], self.ONERF[0:1, 0:8],
                                          self.CEL[0:1, 0:1], None, ALU.mult))
            self.wait(GP, "sv", tkV)
            self.inc(GP.dma_start(out=self.cc_in.ap()[0:1, 0:8], in_=self.RB[0:1, R1:R1 + 8]), "sd", 16)
            GP.wait_ge(self.sem("sd")[0], self.n("sd"))
            self.inc(GP.collective_compute("AllReduce", ALU.max, replica_groups=[list(range(8))],
                                           ins=[self.cc_in.ap().opt()], outs=[self.cc_out.ap().opt()]), "sc", 1)
            GP.wait_ge(self.sem("sc")[0], self.n("sc"))
            gld = self.inc(GP.dma_start(out=self.CEL[0:1, 0:1], in_=self.cc_out.ap()[0:1, 0:1]),
                           "sd", 16)
            self.wait(V, "sd", gld)
            tkV = self.vs(V.tensor_scalar(self.RB[0:1, R1:R1 + 8], self.ONERF[0:1, 0:8],
                                          self.CEL[0:1, 0:1], None, ALU.mult))
            self.wait(S, "sv", tkV)
            S.activation(self.RB[0:1, R2:R2 + 8], self.RB[0:1, R1:R1 + 8], AF.Ln, bias=0.0, scale=1.0)
            tkS = self.inc(S.activation(self.RB[0:1, R2:R2 + 8], self.RB[0:1, R2:R2 + 8], AF.Exp,
                                        bias=0.0, scale=-1.0), "ss", 1)
            self.wait(V, "ss", tkS)
            tkV = self.vs(V.tensor_scalar(self.RB[0:1, R2:R2 + 8], self.RB[0:1, R2:R2 + 8], 127.0,
                                          None, ALU.mult))
            for ci in range(CH):
                self.wait(SY, "sv")
                ld = self.inc(SY.dma_start(out=self.XS[:, :], in_=wsrc[:, ci, :]), "sl", 16)
                self.wait(S, "sl", ld)
                self.wait(S, "sv")
                tkS = self.inc(S.activation(self.QS[:, :], self.XS[:, :], AF.Abs), "ss", 1)
                self.wait(V, "ss", tkS)
                tkV = self.inc(V.tensor_reduce(self.SMALL[:, 32 + ci:33 + ci], self.QS[:, :], AX, ALU.add),
                               "sv", 1)
            self.wait(V, "sv")
            tkV = self.vs(V.tensor_reduce(self.SMALL[:, 25:26], self.SMALL[:, 32:40], AX, ALU.add))
            self.wait(TE, "sv", tkV)
            tkT = self.inc(TE.transpose(pst[0:1, 128:256], self.SMALL[:, 25:26], self.IDF[:]), "st", 1)
            self.wait(V, "st", tkT)
            self.vs(V.tensor_reduce(self.CEL[0:1, 1:2], pst[0:1, 128:256], AX, ALU.add))
            self.vs(V.tensor_scalar(self.RB[0:1, R4:R4 + 8], self.ONERF[0:1, 0:8], self.CEL[0:1, 1:2],
                                    None, ALU.mult))
            self.vs(V.tensor_scalar(self.RB[0:1, R4:R4 + 8], self.RB[0:1, R4:R4 + 8],
                                    1.0 / (1024.0 * 1024.0), None, ALU.mult))
            self.vs(V.tensor_scalar(self.RB[0:1, R5:R5 + 8], self.RB[0:1, R4:R4 + 8], 0.5, None,
                                    ALU.mult))
            self.vs(V.tensor_tensor(self.RB[0:1, R5 + 8:R5 + 16], self.RB[0:1, R4:R4 + 8],
                                    self.RB[0:1, R1:R1 + 8], ALU.mult))
            tkV = self.vs(V.tensor_scalar(self.RB[0:1, R5 + 8:R5 + 16], self.RB[0:1, R5 + 8:R5 + 16],
                                          1.0 / 127.0, None, ALU.mult))
            self.wait(TE, "sv", tkV)
            TE.matmul(psb[:, 0:8], self.ONERF[:], self.RB[0:1, R2:R2 + 8], start=True, stop=True)
            TE.matmul(psb[:, 8:16], self.ONERF[:], self.RB[0:1, R5 + 8:R5 + 16], start=True, stop=True)
            tkT = self.inc(TE.matmul(psb[:, 16:24], self.ONERF[:], self.RB[0:1, R5:R5 + 8],
                                     start=True, stop=True), "st", 1)
            self.wait(V, "st", tkT)
            tkV = self.vs(V.tensor_copy(self.SMALL[:, 0:24], psb[:, 0:24]))
            self.wait(SY, "sv", tkV)
            qwT = self.WBUF[:].rearrange("p (k f) -> p k f", k=8)
            for ci in range(CH):
                self.wait(SY, "sv")
                ld = self.inc(SY.dma_start(out=self.XS[:, :], in_=wsrc[:, ci, :]), "sl", 16)
                self.wait(S, "sl", ld)
                S.activation(self.QS[:, :], self.XS[:, :], AF.Sign)
                tkS = self.inc(S.activation(self.XS[:, :], self.XS[:, :], AF.Abs), "ss", 1)
                self.wait(V, "ss", tkS)
                V.tensor_scalar(self.XS[:, :], self.XS[:, :], self.SMALL[:, 16:17], None, ALU.is_gt)
                tkV = self.inc(V.tensor_tensor(qwT[:, ci, :], self.QS[:, :], self.XS[:, :], ALU.mult), "sv", 1)
        self.gate()
        with ExitStack() as px:
            bc_i = self.pt(px, "bcA", [128, 512], F32)
            bc_c = self.pt(px, "bcB", [128, 512], F32)
            self.ps_row = self.pt(px, "lpr", [1, 512], F32)
            for g in range(4):
                lo = g * 512
                self.wait(TE, "sv")
                tkT = self.ln_apply_bcast(px, g, self.PK_MU1, self.PK_IV1, bc_i, bc_c)
                for ci in range(CH):
                    self.wait(V, "st", tkT)
                    V.tensor_tensor(self.XS[:, 0:512], self.ych(ci)[:, 1 + lo:1 + lo + 512], bc_i[:], ALU.mult)
                    V.tensor_tensor(self.XS[:, 0:512], self.XS[:, 0:512], bc_c[:], ALU.add)
                    V.tensor_tensor(self.ych(ci)[:, 1 + lo:1 + lo + 512],
                                    self.ych(ci)[:, 1 + lo:1 + lo + 512], self.XS[:, 0:512], ALU.add)
                    V.tensor_scalar(self.QS[:, 0:512], self.XS[:, 0:512], self.SMALL[:, 0:1], None, ALU.mult)
                    V.tensor_copy(self.QI8[:, 0:512], self.QS[:, 0:512])
                    V.tensor_copy(self.XS[:, 0:512], self.QI8[:, 0:512])
                    V.tensor_tensor(self.QS[:, 0:512], self.QS[:, 0:512], self.XS[:, 0:512], ALU.subtract)
                    V.scalar_tensor_tensor(self.XS[:, 0:512], self.QS[:, 0:512], 0.5, self.XS[:, 0:512],
                                           ALU.is_ge, ALU.add)
                    V.tensor_scalar(self.QS[:, 0:512], self.QS[:, 0:512], -0.5, None, ALU.is_le)
                    tkV = self.inc(V.tensor_tensor(self.ybf(ci)[:, 1 + lo:1 + lo + 512], self.XS[:, 0:512],
                                                   self.QS[:, 0:512], ALU.subtract), "sv", 1)
                    self.wait(TE, "sv", tkV)
        self.gate()
        qwT = self.WBUF[:].rearrange("p (k f) -> p k f", k=8)
        with ExitStack() as px:
            psf = [self.pt(px, "ff", [128, 512], F32) for i in range(2)]
            for m in range(CH):
                for g in range(4):
                    ps = psf[g % 2]
                    self.wait(TE, "sv")
                    for k in range(CH):
                        mm = TE.matmul(ps[:], qwT[:, k, m * 128:(m + 1) * 128],
                                       self.YBF[:, k * BW + 1 + g * 512: k * BW + 1 + (g + 1) * 512],
                                       start=(k == 0), stop=(k == CH - 1))
                    tkT = self.inc(mm, "st", 1)
                    self.wait(V, "st", tkT)
                    V.tensor_scalar(self.XS[:, 0:512], ps[:], self.SMALL[:, 8:9],
                                    self.spcol("l0_fbias", m), ALU.mult, ALU.add)
                    tkV = self.inc(V.tensor_tensor(self.ych(m)[:, 1 + g * 512:1 + (g + 1) * 512],
                                                   self.ych(m)[:, 1 + g * 512:1 + (g + 1) * 512],
                                                   self.XS[:, 0:512], ALU.add), "sv", 1)
                    self.wait(TE, "sv", tkV)
        self.gate()

    def dump_hb(self):
        V, SY = self.V, self.SY
        ov = self.out.rearrange("(c p) t -> p c t", p=128)
        for ci in range(CH):
            for hh in range(2):
                self.wait(V, "so")
                tkV = self.inc(V.tensor_copy(self.XS[:, :],
                                             self.HB[:, ci, hh * 1024:(hh + 1) * 1024]), "sv", 1)
                self.wait(SY, "sv", tkV)
                self.inc(SY.dma_start(out=ov[:, ci, hh * 1024:(hh + 1) * 1024], in_=self.XS[:, :]), "so", 16)
                self.wait(SY, "so")
        SY.wait_ge(self.sem("so")[0], self.n("so"))

    def dump_dbg(self):
        V, SY = self.V, self.SY
        ov = self.out.rearrange("(c p) t -> p c t", p=128)
        self.wait(SY, "sv")
        self.wait(SY, "ss")
        self.inc(SY.dma_start(out=self.out[0:1, 0:2048], in_=self.RB[0:1, 2048:4096]), "so", 16)
        self.inc(SY.dma_start(out=ov[:, 1, 0:64], in_=self.SMALL[:, :]), "so", 16)
        for cc_ in range(2):
            self.wait(V, "so")
            tkV = self.inc(V.tensor_copy(self.XS[:, 0:1024],
                                         self.ybf(cc_)[:, 1:1025]), "sv", 1)
            self.wait(SY, "sv", tkV)
            self.inc(SY.dma_start(out=ov[:, 2 + cc_, 0:1024], in_=self.XS[:, 0:1024]), "so", 16)
            self.wait(SY, "so")
        qwT = self.WBUF[:].rearrange("p (k f) -> p k f", k=8)
        for cc_ in range(2):
            self.wait(V, "so")
            tkV = self.inc(V.tensor_copy(self.XS[:, 0:1024], qwT[:, cc_, :]), "sv", 1)
            self.wait(SY, "sv", tkV)
            self.inc(SY.dma_start(out=ov[:, 4 + cc_, 0:1024], in_=self.XS[:, 0:1024]), "so", 16)
            self.wait(SY, "so")
        SY.wait_ge(self.sem("so")[0], self.n("so"))

    def dump_y(self):
        ov = self.out.rearrange("(c p) t -> p c t", p=128)
        for ci in range(CH):
            for hh in range(2):
                self.inc(self.SY.dma_start(out=ov[:, ci, hh * 1024:(hh + 1) * 1024],
                                           in_=self.ych(ci)[:, 1 + hh * 1024:1 + (hh + 1) * 1024]), "so", 16)
        self.SY.wait_ge(self.sem("so")[0], self.n("so"))

    def phase5_ln2(self):
        nc, V, SY, TE = self.nc, self.V, self.SY, self.T
        xs = self.xsave.ap().rearrange("(c p) t -> p c t", p=128)
        for ci in range(CH):
            self.inc(SY.dma_start(out=xs[:, ci, :], in_=self.ych(ci)[:, 1:1 + T]), "sx", 16)
        self.ln_stats(self.PK_MU1, self.PK_IV1, shifted=False)
        SY.wait_ge(self.sem("sx")[0], self.n("sx"))
        self.gate()
        with ExitStack() as px:
            bc_i = self.pt(px, "bcA", [128, 512], F32)
            bc_c = self.pt(px, "bcB", [128, 512], F32)
            self.ps_row = self.pt(px, "lpr", [1, 512], F32)
            for g in range(4):
                lo = g * 512
                self.wait(TE, "sv")
                tkT = self.ln_apply_bcast(px, g, self.PK_MU1, self.PK_IV1, bc_i, bc_c)
                for ci in range(CH):
                    self.wait(V, "st", tkT)
                    V.tensor_tensor(self.ych(ci)[:, 1 + lo:1 + lo + 512],
                                    self.ych(ci)[:, 1 + lo:1 + lo + 512], bc_i[:], ALU.mult)
                    V.tensor_tensor(self.ych(ci)[:, 1 + lo:1 + lo + 512],
                                    self.ych(ci)[:, 1 + lo:1 + lo + 512], bc_c[:], ALU.add)
                    tkV = self.inc(V.tensor_copy(self.ybf(ci)[:, 1 + lo:1 + lo + 512],
                                                 self.ych(ci)[:, 1 + lo:1 + lo + 512]), "sv", 1)
                    self.wait(TE, "sv", tkV)
        self.gate()

    def phase6_l1_attn(self):
        nc, V, S, TE, GP, SY = self.nc, self.V, self.S, self.T, self.GP, self.SY
        AB = self.ABUF
        QFM = AB[:, 0:2048]
        KT = AB[:, 2048:4096].rearrange("p (c d) -> p c d", c=16)
        VT = AB[:, 4096:6144].rearrange("p (c d) -> p c d", c=16)
        ATT = self.SHD[:, 0:16384].rearrange("p (c t) -> p c t", c=CH)
        wqv = self.WBUF[:, 0:3072].rearrange("p (w k d) -> p w k d", w=3, k=8)
        wsrc = self.d_l1_wqkvT.rearrange("(k p) f -> p k f", p=128)
        with ExitStack() as px:
            psq = self.pt(px, "laq", [128, 512], F32)
            psk = self.pt(px, "lak", [128, 512], F32)
            psn = self.pt(px, "lan", [64, 512], F32)
            pso = self.pt(px, "lao", [64, 512], F32)
            psd = self.pt(px, "lad", [1, 512], F32)
            psc = self.pt(px, "lac", [128, 128], F32)
            for hp in range(CH):
                self.wait(GP, "st")
                for w, col0 in ((0, hp * 128), (1, C + hp * 128), (2, 2 * C + hp * 128)):
                    self.inc(GP.dma_start(out=wqv[:, w, :, :], in_=wsrc[:, :, col0:col0 + 128]), "sg", 16)
                tw = self.n("sg")
                self.wait(TE, "sg", tw)
                for g in range(4):
                    self.wait(TE, "sv")
                    for k in range(CH):
                        mm = TE.matmul(psq[:], wqv[:, 0, k, :],
                                       self.YBF[:, k * BW + 1 + g * 512: k * BW + 1 + (g + 1) * 512],
                                       start=(k == 0), stop=(k == CH - 1))
                    tkT = self.inc(mm, "st", 1)
                    self.wait(V, "st", tkT)
                    self.inc(V.tensor_relu(QFM[:, g * 512:(g + 1) * 512], psq[:]), "sv", 1)
                for tc in range(16):
                    self.wait(TE, "sv")
                    for k in range(CH):
                        TE.matmul(psk[:, 0:128], self.YBF[:, k * BW + 1 + tc * 128: k * BW + 1 + (tc + 1) * 128],
                                  wqv[:, 1, k, :], start=(k == 0), stop=(k == CH - 1))
                    for k in range(CH):
                        mm = TE.matmul(psk[:, 128:256],
                                       self.YBF[:, k * BW + 1 + tc * 128: k * BW + 1 + (tc + 1) * 128],
                                       wqv[:, 2, k, :], start=(k == 0), stop=(k == CH - 1))
                    tkT = self.inc(mm, "st", 1)
                    self.wait(V, "st", tkT)
                    V.tensor_relu(KT[:, tc, :], psk[:, 0:128])
                    self.inc(V.tensor_copy(VT[:, tc, :], psk[:, 128:256]), "sv", 1)
                self.wait(TE, "sv")
                for tc in range(16):
                    mm = TE.matmul(psc[:, 0:8], KT[:, tc, :], self.ONEB[:, 0:8],
                                   start=(tc == 0), stop=(tc == 15))
                tkT = self.inc(mm, "st", 1)
                self.wait(V, "st", tkT)
                tkV = self.inc(V.tensor_copy(self.KSB[:, 0:8], psc[:, 0:8]), "sv", 1)
                self.wait(TE, "sv", tkV)
                for hh in range(2):
                    h0 = hh * DH
                    for tc in range(16):
                        mm = TE.matmul(psc[0:DH, 0:DH], KT[:, tc, h0:h0 + DH],
                                       VT[:, tc, h0:h0 + DH], start=(tc == 0), stop=(tc == 15))
                    tkT = self.inc(mm, "st", 1)
                    self.wait(V, "st", tkT)
                    tkV = self.inc(V.tensor_copy(self.CTXB[h0:h0 + DH, 0:DH], psc[0:DH, 0:DH]), "sv", 1)
                    self.wait(TE, "sv", tkV)
                for hh in range(2):
                    h0 = hh * DH
                    for g in range(4):
                        self.wait(S, "sv")
                        mm = TE.matmul(psd[:], self.KSB[h0:h0 + DH, 0:1],
                                       QFM[h0:h0 + DH, g * 512:(g + 1) * 512], start=True, stop=True)
                        tkT = self.inc(mm, "st", 1)
                        self.wait(S, "st", tkT)
                        S.activation(self.rb(2), psd[:], AF.Ln, bias=self.CNST[0:1, 16:17], scale=1.0)
                        tkS = self.inc(S.activation(self.rb(2), self.rb(2), AF.Exp, bias=0.0, scale=-1.0),
                                       "ss", 1)
                        self.wait(TE, "ss", tkS)
                        TE.matmul(pso[0:DH, :], self.CTXB[h0:h0 + DH, 0:DH],
                                  QFM[h0:h0 + DH, g * 512:(g + 1) * 512], start=True, stop=True)
                        mm = TE.matmul(psn[0:DH, :], self.ONERF[0:1, 0:DH], self.rb(2), start=True, stop=True)
                        tkT = self.inc(mm, "st", 1)
                        self.wait(V, "st", tkT)
                        V.tensor_copy(self.XS[0:DH, 0:512], psn[0:DH, :])
                        tkV = self.inc(V.tensor_tensor(ATT[h0:h0 + DH, hp, g * 512:(g + 1) * 512],
                                                       pso[0:DH, :], self.XS[0:DH, 0:512], ALU.mult), "sv", 1)
                        self.wait(TE, "sv", tkV)
        self.gate()
        with ExitStack() as px:
            pso2 = [self.pt(px, "pj", [128, 512], F32) for i in range(2)]
            wom = self.WBUF[:, 0:2048].rearrange("p (b k m) -> p b k m", b=2, k=8)
            use_tok = [0] * CH
            wsrc2 = self.d_l1_wprojT.rearrange("(k p) f -> p k f", p=128)
            xsv = self.xsave.ap().rearrange("(c p) t -> p c t", p=128)
            for m in range(CH):
                if m >= 2:
                    self.wait(GP, "st", use_tok[m - 2])
                ld = self.inc(GP.dma_start(out=wom[:, m % 2, :, :], in_=wsrc2[:, :, m * 128:(m + 1) * 128]), "sg", 16)
                self.wait(TE, "sg", ld)
                for g in range(4):
                    ps = pso2[g % 2]
                    self.wait(TE, "sv")
                    for k in range(CH):
                        mm = TE.matmul(ps[:], wom[:, m % 2, k, :], ATT[:, k, g * 512:(g + 1) * 512],
                                       start=(k == 0), stop=(k == CH - 1))
                    tkT = self.inc(mm, "st", 1)
                    use_tok[m] = tkT
                    ld2 = self.inc(SY.dma_start(out=self.QS[:, 0:512], in_=xsv[:, m, g * 512:(g + 1) * 512]),
                                   "sl", 16)
                    self.wait(V, "st", tkT)
                    self.wait(V, "sl", ld2)
                    self.wait(V, "so")
                    V.tensor_scalar(self.XS[:, 0:512], ps[:], self.spcol("l1_bproj", m), None, ALU.add)
                    tkV = self.inc(V.tensor_tensor(self.XS[:, 0:512], self.XS[:, 0:512], self.QS[:, 0:512],
                                                   ALU.add), "sv", 1)
                    self.wait(SY, "sv", tkV)
                    self.inc(SY.dma_start(out=xsv[:, m, g * 512:(g + 1) * 512], in_=self.XS[:, 0:512]),
                             "so", 16)
                    self.wait(SY, "so")
        self.gate()

    def ybf_regen(self):
        V = self.V
        for ci in range(CH):
            for hh in range(2):
                tkV = self.inc(V.tensor_copy(self.ybf(ci)[:, 1 + hh * 1024:1 + (hh + 1) * 1024],
                                             self.ych(ci)[:, 1 + hh * 1024:1 + (hh + 1) * 1024]), "sv", 1)
        self.gate()

    def internal_layer(self, li, do_attn=True, do_ffn=True):
        nc, V, S, TE, GP = self.nc, self.V, self.S, self.T, self.GP
        AB = self.ABUF
        SHD = self.SHD
        QTOK = SHD[:, 0:2048].rearrange("p (c d) -> p c d", c=8)
        KTOK = SHD[:, 2048:4608].rearrange("p (c d) -> p c d", c=10)
        VTOK = SHD[:, 4608:7168].rearrange("p (c d) -> p c d", c=10)
        QFM = SHD[:, 7168:9216].rearrange("p (c t) -> p c t", c=2)
        KFM = SHD[:, 9216:11776].rearrange("p (c t) -> p c t", c=2)
        PB = SHD[:, 11776:12160]
        ATT = AB[:, 0:8192].rearrange("p (c t) -> p c t", c=CH)
        wq3 = self.WBUF[:, 0:6144].rearrange("p (w k d) -> p w k d", w=3, k=8)
        wsrc = self.d_wqkv[li].rearrange("(k p) f -> p k f", p=128)

        if do_attn:
            self.ybf_regen()
            with ExitStack() as px:
                pq = self.pt(px, "ipq", [128, 512], F32)
                pst = self.pt(px, "ipt", [128, 1024], BF16)
                pss = self.pt(px, "ips", [128, 512], F32)
                pvo = [self.pt(px, "ipo", [DH, 128], F32) for i in range(2)]
                for H in range(2):
                    qlo = H * 1024
                    klo = max(0, qlo - 256)
                    nkc = (qlo + 1024 - klo) // 128
                    for g in range(4):
                        hs = g * 4
                        self.wait(GP, "st")
                        toks = []
                        for w in range(3):
                            toks.append(self.inc(GP.dma_start(
                                out=wq3[:, w, :, :], in_=wsrc[:, :, w * C + hs * DH: w * C + hs * DH + 256]), "sg", 16))
                        self._int_qkv(li, H, g, qlo, klo, nkc, pq, pst, QTOK, KTOK, VTOK, QFM, KFM, wq3, toks)
                        self._int_attn(li, H, g, qlo, klo, nkc, pss, pst, pvo, QFM, KFM, VTOK, PB, ATT)
                    self._int_wo(li, H, ATT)
            self.gate()
        if not do_ffn:
            return
        self.ybf_regen()
        self.ln_stats(self.PK_MU1, self.PK_IV1, shifted=True)
        HBF = SHD[:, 0:NJ * 512].rearrange("p (j t) -> p j t", j=NJ)
        XN2 = AB[:, 0:8192].rearrange("p (b c t) -> p b c t", b=2, c=CH)
        w1s = self.WBUF[:, 0:4096].rearrange("p (b x k d) -> p b x k d", b=2, x=2, k=8)
        w2v = self.WBUF[:, 4096:6912].rearrange("p (k d) -> p k d", k=NJ)
        w1src = self.d_w1[li].rearrange("(k p) f -> p k f", p=128)

        def xn_quarter(Q):
            qlo = Q * 512
            with ExitStack() as px:
                bc_i = self.pt(px, "bcA", [128, 512], F32)
                bc_c = self.pt(px, "bcB", [128, 512], F32)
                self.ps_row = self.pt(px, "lpr", [1, 512], F32)
                self.wait(TE, "sv")
                self.wait(TE, "ss")
                tkT = self.ln_apply_bcast(px, Q, self.PK_MU1, self.PK_IV1, bc_i, bc_c)
                for ci in range(CH):
                    off = 0 if ci >= 4 else 1
                    self.wait(V, "st", tkT)
                    self.wait(V, "ss")
                    V.tensor_tensor(self.XS[:, 0:512], self.ych(ci)[:, off + qlo:off + qlo + 512],
                                    bc_i[:], ALU.mult)
                    tkV = self.inc(V.tensor_tensor(self.XS[:, 0:512], self.XS[:, 0:512], bc_c[:], ALU.add),
                                   "sv", 1)
                    self.wait(S, "sv", tkV)
                    self.inc(S.activation(XN2[:, Q % 2, ci, :], self.XS[:, 0:512], AF.Identity,
                                          bias=self.spcol(f"b{li}", ci), scale=self.spcol(f"g{li}", ci)),
                             "ss", 1)
                self.wait(TE, "ss")
                self.wait(V, "ss")

        xn_quarter(0)
        for Q in range(4):
            qlo = Q * 512
            XN = XN2[:, Q % 2]
            with ExitStack() as px:
                pa = [self.pt(px, "ipa", [128, 512], F32) for _ in range(2)]
                pgx = [self.pt(px, "ipg", [128, 512], F32) for _ in range(2)]
                gel_tok = [0] * NJ
                use_tok = [0] * NJ
                for j in range(NJ):
                    jw = 128 if j < NJ - 1 else JL
                    if j >= 2:
                        self.wait(GP, "sv", use_tok[j - 2])
                    self.inc(GP.dma_start(out=w1s[:, j % 2, 0, :, 0:jw],
                                          in_=w1src[:, :, j * 128:j * 128 + jw]), "sg", 16)
                    ld = self.inc(GP.dma_start(out=w1s[:, j % 2, 1, :, 0:jw],
                                               in_=w1src[:, :, D1 + j * 128:D1 + j * 128 + jw]), "sg", 16)
                    self.wait(TE, "sg", ld)
                    if j >= 2:
                        self.wait(TE, "sv", use_tok[j - 2])
                        self.wait(TE, "ss", gel_tok[j - 2])
                    for k in range(CH):
                        TE.matmul(pa[j % 2][0:jw, :], w1s[:, j % 2, 0, k, 0:jw], XN[:, k, 0:512],
                                  start=(k == 0), stop=(k == CH - 1))
                        mm = TE.matmul(pgx[j % 2][0:jw, :], w1s[:, j % 2, 1, k, 0:jw], XN[:, k, 0:512],
                                       start=(k == 0), stop=(k == CH - 1))
                    tkT = self.inc(mm, "st", 1)
                    self.wait(S, "st", tkT)
                    tkS = self.inc(S.activation(self.GEL[0:jw, :], pgx[j % 2][0:jw, :], AF.Gelu,
                                                bias=self.spcol(f"b1g{li}", j, 0, jw), scale=1.0), "ss", 1)
                    gel_tok[j] = tkS
                    self.wait(V, "ss", tkS)
                    tkV = self.inc(V.scalar_tensor_tensor(HBF[0:jw, j, :], pa[j % 2][0:jw, :],
                                                          self.spcol(f"b1a{li}", j, 0, jw),
                                                          self.GEL[0:jw, :], ALU.add, ALU.mult), "sv", 1)
                    use_tok[j] = tkV
                self.wait(TE, "sv")
            if Q < 3:
                xn_quarter(Q + 1)
            with ExitStack() as px:
                pc2 = [self.pt(px, "ipc", [128, 512], F32) for i in range(2)]
                w2full = self.d_w2[li][0:(NJ - 1) * 128, :].rearrange("(k p) f -> p k f", p=128)
                for m in range(CH):
                    self.wait(GP, "st")
                    self.inc(GP.dma_start(out=w2v[:, 0:NJ - 1, :], in_=w2full[:, :, m * 128:(m + 1) * 128]), "sg", 16)
                    ld = self.inc(GP.dma_start(out=w2v[0:JL, NJ - 1, :],
                                               in_=self.d_w2[li][(NJ - 1) * 128: (NJ - 1) * 128 + JL,
                                                                 m * 128:(m + 1) * 128]), "sg", 16)
                    self.wait(TE, "sg", ld)
                    ps = pc2[m % 2]
                    self.wait(TE, "sv")
                    for k in range(NJ):
                        kw = 128 if k < NJ - 1 else JL
                        mm = TE.matmul(ps[:], w2v[0:kw, k, :], HBF[0:kw, k, :],
                                       start=(k == 0), stop=(k == NJ - 1))
                    tkT = self.inc(mm, "st", 1)
                    self.wait(V, "st", tkT)
                    tkV = self.inc(V.scalar_tensor_tensor(self.ych(m)[:, 1 + qlo:1 + qlo + 512], ps[:],
                                                          self.spcol(f"b2_{li}", m),
                                                          self.ych(m)[:, 1 + qlo:1 + qlo + 512],
                                                          ALU.add, ALU.add), "sv", 1)
                    self.wait(TE, "sv", tkV)
        self.gate()

    def _int_qkv(self, li, H, g, qlo, klo, nkc, pq, pst, QTOK, KTOK, VTOK, QFM, KFM, wq3, toks):
        nc, V, S, TE = self.nc, self.V, self.S, self.T
        qs3 = self.qskv[:, li, 0, :].rearrange("p (o d) -> p o d", o=1).broadcast_to([128, 4, DH])
        ks3 = self.qskv[:, li, 1, :].rearrange("p (o d) -> p o d", o=1).broadcast_to([128, 4, DH])

        def proc(w, tok0, ntc, dtok, tbl_a, tbl_b, sc3):
            self.wait(TE, "sg", toks[w])
            for tc in range(ntc):
                t0 = tok0 + tc * 128
                self.wait(TE, "sv")
                self.wait(TE, "ss")
                for k in range(CH):
                    off = 0 if k >= 4 else 1
                    mm = TE.matmul(pq[:, 0:256], self.YBF[:, k * BW + off + t0: k * BW + off + t0 + 128],
                                   wq3[:, w, k, :], start=(k == 0), stop=(k == CH - 1))
                tkT = self.inc(mm, "st", 1)
                self.wait(V, "st", tkT)
                tkV0 = self.inc(V.tensor_copy(self.XS[:, 0:256], pq[:, 0:256]), "sv", 1)
                self.wait(S, "sv", tkV0)
                tkS = self.inc(S.activation(self.XS[:, 256:512], self.XS[:, 0:256], AF.Square), "ss", 1)
                self.wait(V, "ss", tkS)
                x3 = self.XS[:, 0:256].rearrange("p (h d) -> p h d", h=4)
                s3 = self.XS[:, 256:512].rearrange("p (h d) -> p h d", h=4)
                rn3 = self.XS[:, 512:516].rearrange("p (h o) -> p h o", o=1)
                tkV = self.inc(V.tensor_reduce(rn3, s3, AX, ALU.add), "sv", 1)
                self.wait(S, "sv", tkV)
                S.activation(self.XS[:, 512:516], self.XS[:, 512:516], AF.Ln,
                             bias=self.CNST[:, 24:25], scale=1.0)
                tkS = self.inc(S.activation(self.XS[:, 512:516], self.XS[:, 512:516], AF.Exp,
                                            bias=0.0, scale=-0.5), "ss", 1)
                self.wait(V, "ss", tkS)
                tcg = t0 // 128
                ta = self.tblv[:, tbl_a, tcg, :].rearrange("p (o d) -> p o d", o=1).broadcast_to([128, 4, DH])
                tb_lo = self.tblv[:, tbl_b, tcg, 0:32].rearrange("p (o d) -> p o d", o=1).broadcast_to([128, 4, 32])
                tb_hi = self.tblv[:, tbl_b, tcg, 32:DH].rearrange("p (o d) -> p o d", o=1).broadcast_to([128, 4, 32])
                V.tensor_tensor(x3, x3, rn3.broadcast_to([128, 4, DH]), ALU.mult)
                V.tensor_tensor(x3, x3, sc3, ALU.mult)
                V.tensor_tensor(s3[:, :, 0:32], x3[:, :, 32:DH], tb_lo, ALU.mult)
                V.tensor_tensor(s3[:, :, 32:DH], x3[:, :, 0:32], tb_hi, ALU.mult)
                V.tensor_tensor(x3, x3, ta, ALU.mult)
                tkV = self.inc(V.tensor_tensor(dtok[:, tc, :].rearrange("p (h d) -> p h d", h=4),
                                               x3, s3, ALU.add), "sv", 1)
                self.wait(S, "sv", tkV)

        proc(0, qlo, 8, QTOK, 0, 1, qs3)
        proc(1, klo, nkc, KTOK, 2, 3, ks3)
        self.wait(TE, "sg", toks[2])
        for tc in range(nkc):
            t0 = klo + tc * 128
            self.wait(TE, "sv")
            for k in range(CH):
                off = 0 if k >= 4 else 1
                mm = TE.matmul(pq[:, 0:256], self.YBF[:, k * BW + off + t0: k * BW + off + t0 + 128],
                               wq3[:, 2, k, :], start=(k == 0), stop=(k == CH - 1))
            tkT = self.inc(mm, "st", 1)
            self.wait(V, "st", tkT)
            self.inc(V.tensor_copy(VTOK[:, tc, :], pq[:, 0:256]), "sv", 1)
        for src, ntc, dstf in ((QTOK, 8, QFM), (KTOK, nkc, KFM)):
            done = 0
            while done < ntc:
                nb = min(4, ntc - done)
                self.wait(TE, "sv")
                for i in range(nb):
                    for p in range(2):
                        mm = TE.transpose(pst[:, (i * 2 + p) * 128:(i * 2 + p + 1) * 128],
                                          src[:, done + i, p * 128:(p + 1) * 128], self.IDB[:])
                tkT = self.inc(mm, "st", 1)
                self.wait(V, "st", tkT)
                pstv = pst[:, 0:nb * 256].rearrange("p (i x t) -> p i x t", i=nb, x=2)
                dv = dstf[:, :, done * 128:(done + nb) * 128].rearrange("p x (i t) -> p i x t", i=nb)
                tkV = self.inc(V.tensor_copy(dv, pstv), "sv", 1)
                self.wait(TE, "sv", tkV)
                done += nb

    def _int_attn(self, li, H, g, qlo, klo, nkc, pss, pst, pvo, QFM, KFM, VTOK, PB, ATT):
        nc, V, S, TE = self.nc, self.V, self.S, self.T
        hs = g * 4
        for hh in range(4):
            h = hs + hh
            chl, r0 = hh // 2, (hh % 2) * DH
            for qc in range(8):
                qcg = (qlo // 128) + qc
                kcs = [c for c in (qcg - 2, qcg - 1, qcg) if c >= 0]
                w = len(kcs) * 128
                self.wait(TE, "ss")
                nk = len(kcs)
                first = True
                if nk == 3:
                    TE.matmul(pss[:, 0:128], self.IDB[:], self.maskv[:, 0, :], start=True, stop=False)
                    first = False
                TE.matmul(pss[:, w - 128:w], self.IDB[:], self.maskv[:, 1, :], start=first, stop=False)
                for i, kc in enumerate(kcs):
                    kl = kc - klo // 128
                    mm = TE.matmul(pss[:, i * 128:(i + 1) * 128],
                                   QFM[r0:r0 + DH, chl, qc * 128:(qc + 1) * 128],
                                   KFM[r0:r0 + DH, chl, kl * 128:(kl + 1) * 128],
                                   start=False, stop=(i == nk - 1))
                tkT = self.inc(mm, "st", 1)
                self.wait(V, "st", tkT)
                self.wait(V, "ss")
                tkV = self.inc(V.tensor_reduce(self.XS[:, 768:769], pss[:, 0:w], AX, ALU.max, negate=True),
                               "sv", 1)
                self.wait(S, "sv", tkV)
                tkS = self.inc(S.activation(PB[:, 0:w], pss[:, 0:w], AF.Exp,
                                            bias=self.XS[:, 768:769], scale=1.0,
                                            accum_out=self.XS[:, 769:770]), "ss", 1)
                S.activation(self.XS[:, 770:771], self.XS[:, 769:770], AF.Ln, bias=0.0, scale=1.0)
                tkS = self.inc(S.activation(self.XS[:, 770:771], self.XS[:, 770:771], AF.Exp,
                                            bias=0.0, scale=-1.0), "ss", 1)
                self.wait(V, "ss", tkS)
                tkV = self.inc(V.tensor_scalar(PB[:, 0:w], PB[:, 0:w], self.XS[:, 770:771], None, ALU.mult),
                               "sv", 1)
                self.wait(TE, "sv", tkV)
                for i in range(len(kcs)):
                    mm = TE.transpose(pst[:, i * 128:(i + 1) * 128], PB[:, i * 128:(i + 1) * 128], self.IDB[:])
                tkT = self.inc(mm, "st", 1)
                self.wait(V, "st", tkT)
                tkV = self.inc(V.tensor_copy(self.PTS[:, 0:w], pst[:, 0:w]), "sv", 1)
                self.wait(TE, "sv", tkV)
                pvb = pvo[qc % 2]
                for i, kc in enumerate(kcs):
                    kl = kc - klo // 128
                    mm = TE.matmul(pvb[:], VTOK[:, kl, hh * DH:(hh + 1) * DH],
                                   self.PTS[:, i * 128:(i + 1) * 128],
                                   start=(i == 0), stop=(i == len(kcs) - 1))
                tkT = self.inc(mm, "st", 1)
                self.wait(S, "st", tkT)
                self.inc(S.activation(ATT[r0:r0 + DH, h // 2, qlo - qlo + qc * 128: qc * 128 + 128]
                                      if False else ATT[r0:r0 + DH, h // 2, qc * 128:(qc + 1) * 128],
                                      pvb[:], AF.Copy, bias=0.0, scale=1.0), "ss", 1)
            self.wait(TE, "ss")

    def _int_wo(self, li, H, ATT):
        nc, V, TE, GP = self.nc, self.V, self.T, self.GP
        qlo = H * 1024
        with ExitStack() as px:
            pso = [self.pt(px, "iwo", [128, 512], F32) for i in range(2)]
            wom = self.WBUF[:, 6144:8192].rearrange("p (b k m) -> p b k m", b=2, k=8)
            use_tok = [0] * CH
            wsrc = self.d_wo[li].rearrange("(k p) f -> p k f", p=128)
            for m in range(CH):
                if m >= 2:
                    self.wait(GP, "st", use_tok[m - 2])
                else:
                    self.wait(GP, "st")
                ld = self.inc(GP.dma_start(out=wom[:, m % 2, :, :], in_=wsrc[:, :, m * 128:(m + 1) * 128]), "sg", 16)
                self.wait(TE, "sg", ld)
                for tg in range(2):
                    ps = pso[tg]
                    self.wait(TE, "sv")
                    for k in range(CH):
                        mm = TE.matmul(ps[:], wom[:, m % 2, k, :], ATT[:, k, tg * 512:(tg + 1) * 512],
                                       start=(k == 0), stop=(k == CH - 1))
                    tkT = self.inc(mm, "st", 1)
                    use_tok[m] = tkT
                    self.wait(V, "st", tkT)
                    tkV = self.inc(V.tensor_tensor(
                        self.ych(m)[:, 1 + qlo + tg * 512:1 + qlo + (tg + 1) * 512],
                        self.ych(m)[:, 1 + qlo + tg * 512:1 + qlo + (tg + 1) * 512], ps[:], ALU.add), "sv", 1)
                    self.wait(TE, "sv", tkV)

    def phase8_out(self):
        V, SY = self.V, self.SY
        xs = self.xsave.ap().rearrange("(c p) t -> p c t", p=128)
        ov = self.out.rearrange("(c p) t -> p c t", p=128)
        for ci in range(CH):
            for hh in range(2):
                ld = self.inc(SY.dma_start(out=self.XS[:, :], in_=xs[:, ci, hh * 1024:(hh + 1) * 1024]),
                              "sl", 16)
                self.wait(V, "sl", ld)
                self.wait(V, "so")
                tkV = self.inc(V.tensor_tensor(self.XS[:, :], self.XS[:, :],
                                               self.ych(ci)[:, 1 + hh * 1024:1 + (hh + 1) * 1024], ALU.add),
                               "sv", 1)
                self.wait(SY, "sv", tkV)
                self.inc(SY.dma_start(out=ov[:, ci, hh * 1024:(hh + 1) * 1024], in_=self.XS[:, :]), "so", 16)
                self.wait(SY, "so")
        SY.wait_ge(self.sem("so")[0], self.n("so"))

    def build(self):
        self.initial_loads()
        self.phase1()
        if self.stage == 1:
            self.dump_xn()
            self.ctx.close()
            return self.nc
        self.phase2_h()
        if self.stage == 15:
            self.dump_hb()
            self.ctx.close()
            return self.nc
        if self.stage == 16:
            self.phase3_l0_attn()
            self.dump_y()
            self.ctx.close()
            return self.nc
        self.phase4_l0_ffn()
        if self.stage in (21, 22):
            if self.stage == 21:
                self.dump_y()
            else:
                self.dump_dbg()
            self.ctx.close()
            return self.nc
        self.phase3_l0_attn()
        if self.stage == 2:
            self.dump_y()
            self.ctx.close()
            return self.nc
        self.phase5_ln2()
        self.phase6_l1_attn()
        nl = 0 if self.stage == 3 else self.nlayers
        for li in range(nl):
            self.internal_layer(li, do_attn=(self.stage != 6), do_ffn=(self.stage != 5))
        self.phase8_out()
        self.ctx.close()
        return self.nc


def kernel(_stage=4, _nlayers=DEPTH, **inputs):
    sp, spc = _host_smallpack(inputs)
    tbl = _host_tables()
    masks = _host_masks()
    identb = np.ascontiguousarray(np.eye(128, dtype=np.float32).astype(ml_dtypes.bfloat16))
    identf = np.eye(128, dtype=np.float32)
    e8c = np.zeros((1, 64), np.float32)
    for _i in range(8):
        e8c[0, 8 * _i + _i] = 1.0
    qs = _host_qs(inputs)
    x = np.asarray(inputs["L0_attn_x"], np.float32)
    bv = np.ascontiguousarray(np.asarray(inputs["L0_attn_b_qkv"], np.float32)[2 * C:].reshape(1, C))

    mk = MK(_stage, _nlayers, sp.shape[1], spc)
    nc = mk.build()
    common = {
        "l0_wqkv": np.asarray(inputs["L0_attn_w_qkv"], np.float32),
        "l0_wout": np.asarray(inputs["L0_attn_w_out"], np.float32),
        "l0_fwT": np.ascontiguousarray(np.asarray(inputs["L0_ffn_weight"], np.float32).T),
        "l1_wqkvT": np.ascontiguousarray(np.asarray(inputs["L1_attn_w_qkv"], np.float32).T),
        "l1_wprojT": np.ascontiguousarray(np.asarray(inputs["L1_attn_w_proj"], np.float32).T),
        "f_wqkv": np.asarray(inputs["L1_ffn_Wqkv"], np.float32),
        "f_wo": np.asarray(inputs["L1_ffn_Wo"], np.float32),
        "f_w1": np.asarray(inputs["L1_ffn_W1"], np.float32),
        "f_w2": np.asarray(inputs["L1_ffn_W2"], np.float32),
        "tbl": tbl, "masks": masks, "identb": identb, "identf": identf,
        "sp": sp, "qsks": qs, "l0_bv": bv, "e8": e8c,
    }
    in_maps = []
    for c in range(8):
        m = dict(common)
        m["xT"] = np.ascontiguousarray(x[c].T)
        in_maps.append(m)
    res = run_bass_kernel_spmd(nc, in_maps, core_ids=list(range(8)))
    return np.stack([np.asarray(r["out"]).T for r in res.results], 0)
